# version 24
# speedup vs baseline: 1.3753x; 1.0105x over previous
import math
import numpy as np
import ml_dtypes
import concourse.bass as bass
import concourse.mybir as mybir
from concourse.bass_utils import run_bass_kernel_spmd

F32 = mybir.dt.float32
BF16 = mybir.dt.bfloat16
I32 = mybir.dt.int32
I16 = mybir.dt.int16
U8 = mybir.dt.uint8
AF = mybir.ActivationFunctionType
ALU = mybir.AluOpType
AX = mybir.AxisListType

D = 1024
T = 2048
NH = 16
HD = 64
HID = 3072
KCN = T // 128
TCN = T // 128
DKC = D // 128
HCN = HID // 128
DELTA = 50.0
SLOPES = [2.0 ** (-0.5 * (i + 1)) for i in range(NH)]
WIN_C = [min(KCN, int(math.ceil((DELTA / s) / 128.0))) for s in SLOPES]
QS = 0.125
DQ = 1.0 / 127.0

ENGS = ("sync", "gpsimd", "vector", "scalar", "tensor")
STREAMS = {"TE": "tensor", "VE": "vector", "SE": "scalar", "GP": "gpsimd",
           "SD": "sync", "GD": "gpsimd", "OD0": "sync", "OD1": "sync",
           "XD0": "sync", "XD1": "sync", "WS": "sync", "W2": "sync",
           "W3": "sync", "I8": "sync", "I9": "sync", "ID": "sync",
           "CN": "sync", "AS0": "sync", "AS1": "sync",
           "AD0": "gpsimd", "AD1": "gpsimd", "GE": "gpsimd"}

DBG = False


class Bld:

    def __init__(self):
        self.ops = {e: [] for e in ENGS}
        self.cnt = {s: 0 for s in STREAMS}

    def emit(self, eng, fn, waits=(), inc=None, dma=False, keep=False):
        ws = [w for w in waits if w is not None]
        tok, amt = None, 0
        if inc is not None:
            amt = 16 if dma else 1
            self.cnt[inc] += amt
            tok = (inc, self.cnt[inc])
        self.ops[eng].append((ws, fn, tok, amt, keep))
        return tok

    def flush(self, block, sems):
        hw = {e: {s: 0 for s in STREAMS} for e in ENGS}

        def emit_for(name):
            def run(eng):
                for ws, fn, tok, amt, keep in self.ops[name]:
                    need = {}
                    for s, v in ws:
                        if (STREAMS[s] == name and not keep
                                and s in ("TE", "VE", "SE", "GP")):
                            continue
                        if v > hw[name][s]:
                            need[s] = max(need.get(s, 0), v)
                    for s in sorted(need):
                        eng.wait_ge(sems[s], need[s])
                        hw[name][s] = need[s]
                    ins = fn(eng)
                    if tok is not None:
                        assert ins is not None, "op with inc returned no instruction"
                        ins.then_inc(sems[tok[0]], amt)
            return run

        block.sync(emit_for("sync"))
        block.gpsimd(emit_for("gpsimd"))
        block.vector(emit_for("vector"))
        block.scalar(emit_for("scalar"))
        block.tensor(emit_for("tensor"))


def tmax(*toks):
    best = {}
    for t in toks:
        if t is None:
            continue
        items = t if isinstance(t, list) else [t]
        for s, v in items:
            if v > best.get(s, -1):
                best[s] = v
    return [(s, v) for s, v in best.items()]


def build_nc():
    nc = bass.Bass("TRN2", target_bir_lowering=False)

    x_d = nc.dram_tensor("x", [T, D], F32, kind="ExternalInput").ap()
    wq_d = nc.dram_tensor("wq", [D, D], F32, kind="ExternalInput").ap()
    wk_d = nc.dram_tensor("wk", [D, D], F32, kind="ExternalInput").ap()
    wv_d = nc.dram_tensor("wv", [D, D], F32, kind="ExternalInput").ap()
    wo_d = nc.dram_tensor("wo", [D, D], F32, kind="ExternalInput").ap()
    wg_d = nc.dram_tensor("wg", [HID, D], I32, kind="ExternalInput").ap()
    wu_d = nc.dram_tensor("wu", [HID, D], I32, kind="ExternalInput").ap()
    wd_d = nc.dram_tensor("wd", [D, HID], I32, kind="ExternalInput").ap()
    jr_d = nc.dram_tensor("jr", [16, T], F32, kind="ExternalInput").ap()
    sc_d = nc.dram_tensor("sc", [16, 1], F32, kind="ExternalInput").ap()
    mm_d = nc.dram_tensor("maskm", [128, 128], F32, kind="ExternalInput").ap()
    id_d = nc.dram_tensor("ident", [128, 128], BF16, kind="ExternalInput").ap()
    out_d = nc.dram_tensor("out", [T, D], F32, kind="ExternalOutput").ap()
    scr1 = nc.dram_tensor("scr1", [2, T], F32, kind="Internal").ap()
    scr2 = nc.dram_tensor("scr2", [2, T], F32, kind="Internal").ap()
    ck_scr = nc.dram_tensor("ck_scr", [2, NH * T], BF16, kind="Internal").ap()
    cq_scr = nc.dram_tensor("cq_scr", [2, NH * T], BF16, kind="Internal").ap()
    dbg_d = {}
    if DBG:
        for nm, w in [("dbg_q", DKC * T), ("dbg_k", DKC * T), ("dbg_v", TCN * 1040),
                      ("dbg_attn", DKC * T), ("dbg_ln1f", DKC * T),
                      ("dbg_hida", 8 * T), ("dbg_xt", DKC * T)]:
            dbg_d[nm] = nc.dram_tensor(nm, [128, w], BF16, kind="ExternalOutput").ap()

    def sb(name, shape, dt):
        return nc.alloc_sbuf_tensor(name, shape, dt).ap()

    maskM = sb("maskm_sb", [128, 128], F32)
    ident = sb("id_sb", [128, 128], BF16)
    svec = sb("svec", [128, 512], U8)
    scol = svec.bitcast(F32)[0:16, 0:1]
    lnv = sb("lnv", [128, 32], F32)
    lnS = [{k: lnv[:, 16 * p + i:16 * p + i + 1] for i, k in enumerate(
        ("s1", "s2", "mv", "qv", "varv", "stdv", "rstd", "tmpv"))} for p in range(2)]
    epsv = lnv[:, 15:16]
    rsq = sb("rsq", [128, 32], F32)

    S1 = sb("S1", [128, 32768], U8)
    S2 = sb("S2", [128, 32768], U8)
    S3 = sb("S3", [128, 33280], U8)
    S4 = sb("S4", [128, 32768], U8)
    S5 = sb("S5", [128, 32768], U8)
    S6 = sb("S6", [128, 16384], U8)
    S8 = sb("S8", [128, 12288], U8)
    S9 = sb("S9", [128, 12288], U8)
    S10 = sb("S10", [128, 4096], U8)

    def view(region, off, dt, shape):
        es = mybir.dt.size(dt)
        n = 1
        for s in shape[1:]:
            n *= s
        ap = region.bitcast(dt)[0:shape[0], off // es: off // es + n]
        if len(shape) == 2:
            return ap
        spec = "p (" + " ".join(f"d{i}" for i in range(1, len(shape))) + ") -> p " + \
               " ".join(f"d{i}" for i in range(1, len(shape)))
        kw = {f"d{i}": shape[i] for i in range(1, len(shape))}
        return ap.rearrange(spec, **kw)

    xT = view(S1, 0, BF16, [128, DKC, T])
    attn = xT
    hidA = view(S1, 0, BF16, [128, 8, T])
    Qp = view(S2, 0, BF16, [128, DKC, T])
    hidB = view(S2, 0, BF16, [128, 8, T])
    Vtok = view(S3, 0, BF16, [128, TCN, 1040])
    hidC = view(S3, 0, BF16, [128, 8, T])
    Kp = view(S4, 0, BF16, [128, DKC, T])
    ln1f = view(S4, 0, BF16, [128, DKC, T])
    wst = view(S5, 0, F32, [128, 4, D])
    wstq = [view(S5, 0, F32, [128, 2, D]), view(S5, 8192, F32, [128, 2, D]),
            view(S8, 0, F32, [128, 2, D])]
    kst = [view(S5, 16384 + i * 4096, BF16, [68, T]) for i in range(2)]
    qst = [view(S5, 24576 + i * 4096, BF16, [68, T]) for i in range(2)]
    wdTa = view(S5, 0, BF16, [128, 16, D])
    wbf = view(S6, 0, BF16, [128, DKC, D])
    wbfB = view(S3, 0, BF16, [128, DKC, D])
    wdTb = view(S6, 0, BF16, [128, 8, D])
    xst = [view(S8, i * 4096, F32, [128, D]) for i in range(2)]
    res1p = [view(S8, 8192, F32, [128, D]), view(S5, 8192, F32, [128, D])]
    r64s = [view(S8, 0, F32, [128, 2048])[64:65, :],
            view(S9, 4096, F32, [128, 2048])[64:65, :]]
    rbs = [view(S8, 0, F32, [128, 2048])[0:64, :],
           view(S5, 0, F32, [128, 2048])[0:64, :]]
    wgT = [view(S8, i * 2048, BF16, [128, 8, 128]) for i in range(2)]
    wuT = [view(S8, 4096 + i * 2048, BF16, [128, 8, 128]) for i in range(2)]
    wdi = view(S8, 8192, I32, [128, D])
    res2p = [view(S8, 0, F32, [128, D]), view(S9, 8192, F32, [128, D])]
    outst = [view(S8, 4096 + i * 4096, F32, [128, D]) for i in range(2)]
    xc = [view(S9, i * 2048, BF16, [128, D]) for i in range(2)]
    ptb = [[view(S9, 0, BF16, [128, 1024]), view(S8, 8192, BF16, [128, 1024])],
           [view(S9, 2048, BF16, [128, 1024]), view(S8, 10240, BF16, [128, 1024])]]
    astg = [[view(S9, 4096 + i * 4096, BF16, [64, T]),
             view(S5, 8192 + i * 4096, BF16, [64, T])] for i in range(2)]
    ln1sp = [view(S9, 0, BF16, [128, D]), view(S9, 2048, BF16, [128, D])]
    wgc = view(S9, 4096, BF16, [128, D])
    wuc = view(S9, 6144, BF16, [128, D])
    wdc = wgc
    gs = [view(S9, 8192 + i * 1024, BF16, [128, 512]) for i in range(2)]
    us = [view(S9, 10240 + i * 1024, BF16, [128, 512]) for i in range(2)]
    ln1tcp = [view(S9, 0, BF16, [128, D]), view(S9, 2048, BF16, [128, D])]
    jrow16 = view(S5, 16384, F32, [16, T])
    cvhS = view(S9, 4096, BF16, [16, T])
    cvlS = view(S9, 8192, BF16, [16, T])

    psA = nc.alloc_psum_tensor("psA", [128, 2048], F32).ap()
    psB = nc.alloc_psum_tensor("psB", [128, 1024], F32).ap()
    psC = nc.alloc_psum_tensor("psC", [128, 1024], F32).ap()
    psAb = psA.bitcast(BF16)

    b = Bld()

    c1 = b.emit("sync", lambda e: e.dma_start(out=maskM, in_=mm_d), inc="CN", dma=True)
    c2 = b.emit("sync", lambda e: e.dma_start(out=ident, in_=id_d), inc="CN", dma=True)
    c3 = b.emit("sync", lambda e: e.dma_start(out=jrow16, in_=jr_d), inc="CN", dma=True)
    c4 = b.emit("sync", lambda e: e.dma_start(out=scol, in_=sc_d), inc="CN", dma=True)
    t_cn = tmax(c1, c2, c3, c4)
    t_mm_ld = t_id = t_cn
    t_eps = b.emit("gpsimd", lambda e: e.memset(epsv, 1e-5), inc="GP")
    vones = Vtok.rearrange("p t (h c) -> p t h c", c=65)[:, :, :, 64]
    t_vones_a = b.emit("gpsimd", lambda e: e.memset(vones[:, 8:16, :], 1.0), inc="GP")
    for i_ in range(2):
        b.emit("gpsimd", lambda e, i_=i_: e.memset(kst[i_][64:68, :], 1.0), inc="GP")
        t_ones_m = b.emit("gpsimd", lambda e, i_=i_: e.memset(qst[i_][64:68, :], 1.0), inc="GP")
    t_v0 = b.emit("vector", lambda e: e.tensor_scalar(
        out=jrow16, in0=jrow16, scalar1=scol, scalar2=None, op0=ALU.mult),
        waits=t_cn, inc="VE")
    t_cvh = b.emit("vector", lambda e: e.tensor_copy(out=cvhS, in_=jrow16),
                   waits=(), inc="VE")
    t_cvl = b.emit("vector", lambda e: e.tensor_tensor(
        out=cvlS, in0=jrow16, in1=cvhS, op=ALU.subtract), waits=(), inc="VE")
    ckv = [ck_scr[r:r + 1, :].rearrange("a (h t) -> (a h) t", h=16) for r in range(2)]
    cqv = [cq_scr[r:r + 1, :].rearrange("a (h t) -> (a h) t", h=16) for r in range(2)]
    g_ck0 = b.emit("gpsimd", lambda e: e.dma_start(out=ckv[0], in_=cvhS),
                   waits=tmax(t_cvh), inc="GD", dma=True)
    g_ck1 = b.emit("gpsimd", lambda e: e.dma_start(out=ckv[1], in_=cvlS),
                   waits=tmax(t_cvl, g_ck0), inc="GD", dma=True)
    t_ng1 = b.emit("vector", lambda e: e.tensor_scalar(
        out=cvhS, in0=cvhS, scalar1=-1.0, scalar2=None, op0=ALU.mult),
        waits=tmax(g_ck0), inc="VE")
    t_ng2 = b.emit("vector", lambda e: e.tensor_scalar(
        out=cvlS, in0=cvlS, scalar1=-1.0, scalar2=None, op0=ALU.mult),
        waits=tmax(g_ck1), inc="VE")
    g_cq0 = b.emit("gpsimd", lambda e: e.dma_start(out=cqv[0], in_=cvhS),
                   waits=tmax(t_ng1, g_ck1), inc="GD", dma=True)
    g_cq1 = b.emit("gpsimd", lambda e: e.dma_start(out=cqv[1], in_=cvlS),
                   waits=tmax(t_ng2, g_cq0), inc="GD", dma=True)
    t_ck_ready = tmax(g_cq1, t_ones_m)

    t_xdma = [None, None]
    t_xcfree = [None, None]
    t_xcast = [None, None]
    t_xev = [None] * 4
    t_xev_tc = [None] * TCN
    t_xT_last = None
    for tc in range(TCN):
        sl = tc % 2
        bank = tc % 4
        t_xdma[sl] = b.emit(
            "sync", lambda e, tc=tc, sl=sl: e.dma_start(
                out=xst[sl], in_=x_d[128 * tc:128 * (tc + 1), :]),
            waits=tmax(t_xcast[sl]), inc="XD%d" % sl, dma=True)
        t_xcast[sl] = b.emit(
            "vector", lambda e, sl=sl: e.tensor_copy(out=xc[sl], in_=xst[sl]),
            waits=tmax(t_xdma[sl], t_xcfree[sl]), inc="VE")
        tt = None
        for kc in range(DKC):
            tt = b.emit(
                "tensor",
                lambda e, sl=sl, kc=kc, bank=bank: e.transpose(
                    psAb[:, 1024 * bank + 128 * kc:1024 * bank + 128 * (kc + 1)],
                    xc[sl][:, 128 * kc:128 * (kc + 1)], ident),
                waits=tmax(t_xcast[sl], t_id, t_xev[bank]) if kc == 0 else (),
                inc="TE" if kc == DKC - 1 else None)
        t_xcfree[sl] = tt
        psrc = psAb[:, 1024 * bank:1024 * (bank + 1)].rearrange("p (c k) -> p c k", c=DKC)
        t_xev[bank] = b.emit(
            "vector", lambda e, tc=tc, psrc=psrc: e.tensor_copy(
                out=xT[:, :, 128 * tc:128 * (tc + 1)], in_=psrc),
            waits=tmax(tt), inc="VE")
        t_xev_tc[tc] = t_xev[bank]
        t_xT_last = t_xev[bank]
    t_xq = [tmax(*t_xev_tc[4 * nt:4 * nt + 4]) for nt in range(4)]

    t_wcast = None
    wq_streams = ["WS", "W2", "W3"]
    t_wbuf = [None, None, tmax(t_xcast[0], t_xcast[1])]
    pslots = [psB[:, 0:512], psB[:, 512:1024], psC[:, 0:512], psC[:, 512:1024]]
    t_pfree = [tmax(t_xev[0], t_xev[1]), tmax(t_xev[0], t_xev[1]),
               tmax(t_xev[2], t_xev[3]), tmax(t_xev[2], t_xev[3])]
    slot_i = 0

    def proj_w_load(w_dram, prev_reader_tok, nbuf=3, wb=None):
        nonlocal t_wcast
        last = None
        for q in range(4):
            buf = q % nbuf
            t_wdma = b.emit(
                "sync", lambda e, w=w_dram, q=q, buf=buf: e.dma_start(
                    out=wstq[buf],
                    in_=w[256 * q:256 * (q + 1), :].rearrange("(c p) k -> p c k", p=128)),
                waits=tmax(t_wbuf[buf]), inc=wq_streams[buf], dma=True)
            for j in range(2):
                kc = 2 * q + j
                last = b.emit(
                    "vector", lambda e, j=j, kc=kc, buf=buf, wb=wb: e.tensor_copy(
                        out=wb[:, kc, :], in_=wstq[buf][:, j, :]),
                    waits=tmax(t_wdma, prev_reader_tok if q == 0 else None)
                          if j == 0 else (),
                    inc="VE" if j == 1 else None)
            t_wbuf[buf] = last
        t_wcast = last
        return last

    def feat_proj(dst, scale, wcast_tok, wb):
        nonlocal slot_i
        last_ev, last_mm = None, None
        for nt in range(4):
            for mc in range(DKC):
                sl = slot_i % 4
                slot_i += 1
                mm = None
                for kc in range(DKC):
                    w0 = tmax(wcast_tok, t_xq[nt], t_pfree[sl]) if kc == 0 else ()
                    mm = b.emit(
                        "tensor",
                        lambda e, mc=mc, nt=nt, kc=kc, sl=sl, wb=wb: e.matmul(
                            pslots[sl], wb[:, kc, 128 * mc:128 * (mc + 1)],
                            xT[:, kc, 512 * nt:512 * (nt + 1)],
                            start=(kc == 0), stop=(kc == DKC - 1)),
                        waits=w0, inc="TE" if kc == DKC - 1 else None)
                last_mm = mm
                ev = b.emit(
                    "scalar",
                    lambda e, mc=mc, nt=nt, sl=sl, dst=dst, scale=scale: e.activation(
                        out=dst[:, mc, 512 * nt:512 * (nt + 1)], in_=pslots[sl],
                        func=AF.Copy, scale=scale),
                    waits=tmax(mm), inc="SE")
                t_pfree[sl] = [ev]
                last_ev = ev
        return last_ev, last_mm

    cast_q = proj_w_load(wq_d, None, nbuf=2, wb=wbf)
    t_q_ready, mm_q = feat_proj(Qp, QS, cast_q, wbf)
    cast_k = proj_w_load(wk_d, None, wb=wbfB)
    t_k_ready, mm_k = feat_proj(Kp, 1.0, cast_k, wbfB)
    cast_v = proj_w_load(wv_d, mm_q, wb=wbf)
    t_vones_b = b.emit("gpsimd", lambda e: e.memset(vones[:, 0:8, :], 1.0),
                       waits=tmax(mm_k), inc="GP")
    t_vones = tmax(t_vones_a, t_vones_b)
    t_v_ready = None
    mm_v = None
    for tc in range(TCN):
        for dn in range(2):
            sl = slot_i % 4
            slot_i += 1
            mm = None
            for kc in range(DKC):
                w0 = tmax(cast_v, t_xev_tc[tc], t_vones, t_pfree[sl]) if kc == 0 else ()
                mm = b.emit(
                    "tensor",
                    lambda e, tc=tc, dn=dn, kc=kc, sl=sl: e.matmul(
                        pslots[sl], xT[:, kc, 128 * tc:128 * (tc + 1)],
                        wbf[:, kc, 512 * dn:512 * (dn + 1)],
                        start=(kc == 0), stop=(kc == DKC - 1)),
                    waits=w0, inc="TE" if kc == DKC - 1 else None)
            mm_v = mm
            dst = Vtok[:, tc, :].rearrange("p (h c) -> p h c", c=65)[:, 8 * dn:8 * (dn + 1), 0:64]
            ev = b.emit(
                "vector", lambda e, sl=sl, dst=dst: e.tensor_copy(
                    out=dst, in_=pslots[sl].rearrange("p (h c) -> p h c", c=64)),
                waits=tmax(mm), inc="VE")
            t_pfree[sl] = [ev]
            t_v_ready = ev
    cast_o = proj_w_load(wo_d, mm_v, nbuf=2, wb=wbf)

    slotpsum = [psB, psC]
    t_kq_used = [tmax(t_cvl), tmax(t_cvl)]
    t_pt_free = [[None, None], [None, None]]
    pk_ctr = [0, 0]
    t_strip_free = [t_pfree[0] + t_pfree[1], t_pfree[2] + t_pfree[3]]
    t_pvevac = [tmax(t_xev[0], t_xev[1], t_v_ready, t_vones),
                tmax(t_xev[2], t_xev[3], t_v_ready, t_vones)]
    t_astg_free = [[tmax(g_cq0), tmax(cast_o)], [tmax(g_cq1), tmax(cast_o)]]
    t_attn_done = []
    t_attn_pair = [None] * DKC
    t_asm_s = [None, None]
    t_last_pv = [None, None]
    fin_q = [[], []]
    t_d2_tok = [None, None]
    t_mult_last = [None, None]
    t_r64_free = [None, None]
    t_rbs_free = [None, None]
    t_gd_prev = [tmax(g_cq1), tmax(g_cq1)]
    t_ad_prev = [None, None]
    GDS = ["GD", "GE"]

    def head_strips(h, v):
        out = []
        for kc in range(KCN):
            so = max(128 * kc, 1024 * v)
            qhi = min(T, 128 * (kc + 1 + WIN_C[h]), 1024 * (v + 1))
            if so < qhi:
                out.append((kc, so, qhi - so))
        return out

    def emit_asm(h):
        par = h % 2
        hch, hrow = h // 2, 64 * (h % 2)
        asm_w = tmax(t_kq_used[par], t_k_ready, t_q_ready, t_ck_ready)
        if hrow == 0:
            a1 = b.emit("vector", lambda e, par=par, hch=hch: e.tensor_copy(
                out=kst[par][0:64, :], in_=Kp[0:64, hch, :]), waits=asm_w, inc="VE")
            a2 = b.emit("vector", lambda e, par=par, hch=hch: e.tensor_copy(
                out=qst[par][0:64, :], in_=Qp[0:64, hch, :]), waits=asm_w, inc="VE")
        else:
            a1 = b.emit("sync", lambda e, par=par, hch=hch: e.dma_start(
                out=kst[par][0:64, :], in_=Kp[64:128, hch, :]), waits=asm_w,
                inc="AS%d" % par, dma=True)
            a2 = b.emit("sync", lambda e, par=par, hch=hch: e.dma_start(
                out=qst[par][0:64, :], in_=Qp[64:128, hch, :]), waits=asm_w,
                inc="AS%d" % par, dma=True)
        a3 = b.emit("sync", lambda e, par=par, h=h: e.dma_start(
            out=kst[par][64:66, :], in_=ck_scr[:, T * h:T * (h + 1)]), waits=asm_w,
            inc="AS%d" % par, dma=True)
        a4 = b.emit("sync", lambda e, par=par, h=h: e.dma_start(
            out=qst[par][66:68, :], in_=cq_scr[:, T * h:T * (h + 1)]), waits=asm_w,
            inc="AS%d" % par, dma=True)
        t_asm_s[par] = tmax(a1, a2, a3, a4)

    def head_steps(h):
        sl = h % 2
        sp = slotpsum[sl]
        hch, hrow = h // 2, 64 * (h % 2)
        for v in range(2):
            strips = head_strips(h, v)
            packs = []
            cur, used = [], 0
            for (kc, so, W) in strips:
                if used + W > 1024:
                    packs.append((cur, used))
                    cur, used = [], 0
                cur.append((kc, so, W, used))
                used += W
            if cur:
                packs.append((cur, used))
            started = set()
            np_ = len(packs)
            t_exp = [None] * np_
            pars = []
            for _ in range(np_):
                pars.append(pk_ctr[sl] % 2)
                pk_ctr[sl] += 1
            pv_next = [0]

            def emit_pv(i, v=v, packs=packs, started=started,
                        t_exp=t_exp, pars=pars):
                ptap = ptb[sl][pars[i]]
                tok = None
                for (kc, so, W, off) in packs[i][0]:
                    a = so
                    while a < so + W:
                        bk = (a - 1024 * v) // 512
                        e_ = min(so + W, 512 * (a // 512 + 1))
                        st = bk not in started
                        started.add(bk)
                        w0 = tmax(t_exp[i],
                                  t_pvevac[sl] if st else None)
                        tok = b.emit(
                            "tensor",
                            lambda e, h=h, kc=kc, a=a, e_=e_, so=so, off=off,
                            v=v, sl=sl, st=st, ptap=ptap: e.matmul(
                                psA[0:65, a - 1024 * v + 1024 * sl:
                                    e_ - 1024 * v + 1024 * sl],
                                Vtok[:, kc, 65 * h:65 * h + 65],
                                ptap[:, off + a - so:off + e_ - so],
                                start=st, stop=True),
                            waits=w0,
                            inc="TE" if (kc, so, W, off) == packs[i][0][-1]
                                 and e_ >= so + W else None)
                        a = e_
                t_last_pv[sl] = tok
                t_pt_free[sl][pars[i]] = tok

            def emit_pv_upto(j, emit_pv=emit_pv, pv_next=pv_next):
                while pv_next[0] <= j:
                    emit_pv(pv_next[0])
                    pv_next[0] += 1

            for i in range(np_):
                def work(i=i, v=v, packs=packs, t_exp=t_exp, pars=pars,
                         emit_pv_upto=emit_pv_upto):
                    emit_pv_upto(i - 2)
                    sub, used = packs[i]
                    pieces = []
                    for (kc, so, W, off) in sub:
                        a = off
                        while a < off + W:
                            e_ = min(off + W, 512 * (a // 512 + 1))
                            pieces.append((kc, so, off, a, e_))
                            a = e_
                    mm = None
                    for pi, (kc, so, off, a, e_) in enumerate(pieces):
                        w0 = tmax(t_asm_s[sl], t_strip_free[sl]) if pi == 0 else ()
                        mm = b.emit(
                            "tensor",
                            lambda e, kc=kc, so=so, off=off, a=a, e_=e_,
                            sl=sl, sp=sp: e.matmul(
                                sp[:, a:e_],
                                kst[sl][:, 128 * kc:128 * (kc + 1)],
                                qst[sl][:, so + a - off:so + e_ - off],
                                start=True, stop=True),
                            waits=w0, inc="TE" if pi == len(pieces) - 1 else None)
                    t_kq_used[sl] = mm
                    mt = None
                    for (kc, so, W, off) in sub:
                        if so == 128 * kc:
                            mt = b.emit(
                                "vector", lambda e, off=off, sp=sp: e.tensor_tensor(
                                    out=sp[:, off:off + 128],
                                    in0=sp[:, off:off + 128],
                                    in1=maskM, op=ALU.add),
                                waits=tmax(mm, t_mm_ld), inc="VE")
                    ptap = ptb[sl][pars[i]]
                    t_exp[i] = b.emit(
                        "scalar", lambda e, used=used, sp=sp, ptap=ptap: e.activation(
                            out=ptap[:, 0:used], in_=sp[:, 0:used], func=AF.Exp),
                        waits=tmax(mm, mt, t_pt_free[sl][pars[i]]), inc="SE")
                    t_strip_free[sl] = t_exp[i]
                yield work

            def tail(v=v, np_=np_, emit_pv_upto=emit_pv_upto):
                if fin_q[sl]:
                    fin_q[sl].pop(0)()
                emit_pv_upto(np_ - 1)
                off2 = 1024 * sl
                ab = HPOS[h] % 2
                A = astg[sl][ab]
                ev1 = b.emit(
                    "vector", lambda e, v=v, off2=off2, A=A: e.tensor_copy(
                        out=A[:, 1024 * v:1024 * (v + 1)],
                        in_=psA[0:64, off2:off2 + 1024]),
                    waits=tmax(t_last_pv[sl],
                               t_astg_free[sl][ab] if v == 0 else None), inc="VE")
                ev_s = b.emit(
                    "scalar", lambda e, v=v, off2=off2: e.activation(
                        out=r64s[sl][:, 1024 * v:1024 * (v + 1)],
                        in_=psA[64:65, off2:off2 + 1024], func=AF.Copy),
                    waits=tmax(t_last_pv[sl],
                               t_r64_free[sl] if v == 0 else None), inc="SE")
                t_pvevac[sl] = tmax(ev1, ev_s)
                if v == 0:
                    return
                if HPOS[h] + 1 < len(HORD[sl]):
                    emit_asm(HORD[sl][HPOS[h] + 1])
                d1 = b.emit("gpsimd", lambda e: e.dma_start(
                    out=scr1[sl:sl + 1, :], in_=r64s[sl]),
                    waits=tmax(ev_s, t_gd_prev[sl]), inc=GDS[sl], dma=True)
                t_r64_free[sl] = d1
                d1b = b.emit("gpsimd", lambda e: e.dma_start(
                    out=rsq[:, 16 * sl:16 * sl + 16],
                    in_=scr1[sl:sl + 1, :].rearrange("a (p f) -> (a p) f", p=128)),
                    waits=tmax(d1), inc=GDS[sl], dma=True)
                t_gd_prev[sl] = d1b

                def stageA(h=h, d1b=d1b):
                    rc = b.emit("vector", lambda e: e.reciprocal(
                        rsq[:, 16 * sl:16 * sl + 16], rsq[:, 16 * sl:16 * sl + 16]),
                        waits=tmax(d1b), inc="VE")
                    d2a = b.emit("gpsimd", lambda e: e.dma_start(
                        out=scr2[sl:sl + 1, :].rearrange("a (p f) -> (a p) f", p=128),
                        in_=rsq[:, 16 * sl:16 * sl + 16]),
                        waits=tmax(rc, t_gd_prev[sl]), inc=GDS[sl], dma=True)
                    d2w = tmax(d2a, t_rbs_free[sl],
                               cast_o if h >= NH - 2 else None)
                    d2 = b.emit("gpsimd", lambda e: e.dma_start(
                        out=rbs[sl], in_=scr2[sl:sl + 1, :].partition_broadcast(64)),
                        waits=d2w, inc=GDS[sl], dma=True)
                    t_gd_prev[sl] = d2
                    t_d2_tok[sl] = d2

                def stageB(h=h, ab=ab, A=A):
                    hch2, hrow2 = h // 2, 64 * (h % 2)
                    d2 = t_d2_tok[sl]
                    if hrow2 == 0:
                        mlt = b.emit("vector", lambda e, hch2=hch2, A=A: e.tensor_tensor(
                            out=attn[0:64, hch2, :], in0=A, in1=rbs[sl],
                            op=ALU.mult),
                            waits=tmax(d2), inc="VE")
                        t_astg_free[sl][ab] = mlt
                        done = mlt
                    else:
                        mlt = b.emit("vector", lambda e, A=A: e.tensor_tensor(
                            out=A, in0=A, in1=rbs[sl],
                            op=ALU.mult),
                            waits=tmax(d2), inc="VE")
                        ad = b.emit("gpsimd", lambda e, hrow2=hrow2, hch2=hch2, A=A: e.dma_start(
                            out=attn[hrow2:hrow2 + 64, hch2, :], in_=A),
                            waits=tmax(mlt, t_ad_prev[sl]), keep=True,
                            inc="AD%d" % sl, dma=True)
                        t_ad_prev[sl] = ad
                        t_astg_free[sl][ab] = ad
                        done = ad
                    t_attn_done.append(done)
                    t_attn_pair[hch2] = tmax(t_attn_pair[hch2], done)
                    t_mult_last[sl] = mlt
                    t_rbs_free[sl] = mlt
                fin_q[sl].append(stageA)
                fin_q[sl].append(stageB)
            yield tail

    def slot_order(sl0):
        big = list(range(NH - 2 + sl0, sl0 - 1, -4))
        small = list(range(sl0, NH, 4))
        out = []
        for i in range(max(len(big), len(small))):
            if i < len(big):
                out.append(big[i])
            if i < len(small):
                out.append(small[i])
        return out

    HORD = [slot_order(0), slot_order(1)]
    HPOS = {h: i for o in HORD for i, h in enumerate(o)}

    def slot_stream(sl0):
        for h in HORD[sl0]:
            for step in head_steps(h):
                yield step

    emit_asm(NH - 2)
    emit_asm(NH - 1)
    gens = [slot_stream(0), slot_stream(1)]
    alive = [True, True]
    while any(alive):
        for gi in range(2):
            if alive[gi]:
                try:
                    next(gens[gi])()
                except StopIteration:
                    alive[gi] = False
    while fin_q[0] or fin_q[1]:
        for sl in range(2):
            if fin_q[sl]:
                fin_q[sl].pop(0)()
    t_attn_all = tmax(*t_attn_done)
    t_ptfree = t_pt_free
    t_sfree = t_strip_free
    t_pvfree = tmax(t_pvevac[0], t_pvevac[1])
    t_m_prev = tmax(t_mult_last[0], t_mult_last[1])

    def ln_chain(par, res, t_r, t_sq_out, t_sq_waits, out_ap, t_out_waits):
        V = lnS[par]
        t_s1t = b.emit("vector", lambda e, res=res, V=V: e.reduce_sum(
            V["s1"], res, axis=AX.X), waits=(), inc="VE")
        t_s2t = b.emit("scalar", lambda e, res=res, V=V, t_sq_out=t_sq_out: e.activation(
            out=t_sq_out, in_=res, func=AF.Square, accum_out=V["s2"]),
            waits=tmax(t_r, *t_sq_waits), inc="SE")
        t_m = b.emit("vector", lambda e, V=V: e.tensor_scalar(
            out=V["mv"], in0=V["s1"], scalar1=1.0 / D, scalar2=None, op0=ALU.mult),
            waits=tmax(t_s1t, t_s2t), keep=True, inc="VE")
        t_mm2 = b.emit("vector", lambda e, V=V: e.tensor_tensor(
            out=V["tmpv"], in0=V["mv"], in1=V["mv"], op=ALU.mult),
            waits=tmax(t_m), keep=True, inc="VE")
        t_var = b.emit("vector", lambda e, V=V: e.scalar_tensor_tensor(
            out=V["varv"], in0=V["s2"], scalar=1.0 / D, in1=V["tmpv"],
            op0=ALU.mult, op1=ALU.subtract),
            waits=tmax(t_mm2), keep=True, inc="VE")
        t_std = b.emit("scalar", lambda e, V=V: e.activation(
            out=V["stdv"], in_=V["varv"], func=AF.Sqrt, bias=epsv),
            waits=tmax(t_var, t_eps), inc="SE")
        t_rs = b.emit("vector", lambda e, V=V: e.reciprocal(V["rstd"], V["stdv"]),
                      waits=tmax(t_std), inc="VE")
        t_ln = b.emit("vector", lambda e, res=res, V=V, out_ap=out_ap: e.tensor_scalar(
            out=out_ap, in0=res, scalar1=V["mv"], scalar2=V["rstd"],
            op0=ALU.subtract, op1=ALU.mult),
            waits=tmax(t_rs, *t_out_waits), keep=True, inc="VE")
        return t_ln, t_s1t, t_s2t

    t_res_used = [tmax(t_pvfree, t_m_prev), tmax(t_pvfree, t_m_prev)]
    t_res1_free = [tmax(t_pt_free[0][1], t_pt_free[1][1]), tmax(t_astg_free[0][1])]
    t_lnT_bank = [None] * 4
    t_opfree = [tmax(t_sfree[0]), tmax(t_sfree[1])]
    t_ln1s_free = [tmax(t_pt_free[0][0]), tmax(t_pt_free[1][0])]
    t_ln1f_evs = []
    t_s2_prev = [None, None]
    t_ln_prev = [None, None]
    t_r_last = [None, None]
    pend3 = [None]
    for tc in range(TCN):
        sl = tc % 2
        bank = tc % 4
        res1 = res1p[sl]
        ln1s = ln1sp[sl]
        t_xd = b.emit(
            "sync", lambda e, tc=tc, sl=sl: e.dma_start(
                out=xst[sl], in_=x_d[128 * tc:128 * (tc + 1), :]),
            waits=tmax(t_res_used[sl]), inc="XD%d" % sl, dma=True)
        opsum = psB if tc % 2 == 0 else psC
        mm = None
        for dn in range(2):
            for ki in range(DKC):
                kc = HORD[0][ki] // 2
                w0 = tmax(t_attn_pair[kc])
                if dn == 0 and ki == 0:
                    w0 = tmax(w0, cast_o, t_opfree[tc % 2])
                mm = b.emit(
                    "tensor", lambda e, tc=tc, dn=dn, kc=kc, ki=ki, opsum=opsum: e.matmul(
                        opsum[:, 512 * dn:512 * (dn + 1)],
                        attn[:, kc, 128 * tc:128 * (tc + 1)],
                        wbf[:, kc, 512 * dn:512 * (dn + 1)],
                        start=(ki == 0), stop=(ki == DKC - 1)),
                    waits=w0, inc="TE" if (dn == 1 and ki == DKC - 1) else None)
        if pend3[0] is not None:
            pend3[0]()
            pend3[0] = None
        t_r = b.emit(
            "vector", lambda e, sl=sl, opsum=opsum, res1=res1: e.scalar_tensor_tensor(
                out=res1, in0=opsum, scalar=1.0, in1=xst[sl],
                op0=ALU.mult, op1=ALU.add),
            waits=tmax(mm, t_xd, t_res1_free[sl]), inc="VE")
        t_r_last[sl] = t_r
        t_opfree[tc % 2] = tmax(t_r)
        t_ln, t_s1t, t_s2t = ln_chain(
            sl, res1, t_r, xst[sl], [t_s2_prev[sl]], ln1s, [t_ln1s_free[sl]])
        t_s2_prev[sl] = t_s2t
        t_ln_prev[sl] = t_ln
        t_res1_free[sl] = t_ln
        t_res_used[sl] = tmax(t_r, t_s2t)

        def tpose3(tc=tc, sl=sl, bank=bank, ln1s=ln1s, t_ln=t_ln):
            tt = None
            for kc in range(DKC):
                tt = b.emit(
                    "tensor", lambda e, kc=kc, bank=bank, ln1s=ln1s: e.transpose(
                        psAb[:, 1024 * bank + 128 * kc:1024 * bank + 128 * (kc + 1)],
                        ln1s[:, 128 * kc:128 * (kc + 1)], ident),
                    waits=tmax(t_ln, t_lnT_bank[bank]) if kc == 0 else (),
                    inc="TE" if kc == DKC - 1 else None)
            t_ln1s_free[sl] = tt
            psrc = psAb[:, 1024 * bank:1024 * (bank + 1)].rearrange("p (c k) -> p c k", c=DKC)
            ev = b.emit(
                "vector", lambda e, tc=tc, psrc=psrc: e.tensor_copy(
                    out=ln1f[:, :, 128 * tc:128 * (tc + 1)], in_=psrc),
                waits=tmax(tt), inc="VE")
            t_lnT_bank[bank] = ev
            t_ln1f_evs.append(ev)
        pend3[0] = tpose3
    pend3[0]()
    pend3[0] = None
    t_ln1f_all = tmax(*t_ln1f_evs)
    t_ln1f_last = t_ln1f_evs[-1]
    t_ph3_se = tmax(t_s2_prev[0], t_s2_prev[1])
    t_ph3_ve = tmax(t_ln_prev[0], t_ln_prev[1])

    t_wgi_free = tmax(t_r_last[0], t_r_last[1], t_ph3_se, t_ph3_ve)
    t_gslot = [None, None]
    t_uslot = [None, None]
    t_trA = [tmax(t_lnT_bank[i]) for i in range(4)]
    t_hid_toks = []
    t_wdT_toks = []
    t_gs_free = [tmax(*t_astg_free[1]), tmax(*t_astg_free[1])]
    t_us_free = [tmax(*t_astg_free[1]), tmax(*t_astg_free[1])]
    t_wgc_free = tmax(t_astg_free[0][0], t_m_prev)
    t_wuc_free = tmax(t_astg_free[0][0], t_m_prev)
    t_wgT_used = [tmax(t_ph3_se, t_r_last[0], t_r_last[1]) for _ in range(2)]
    t_wuT_used = [tmax(t_ph3_se, t_r_last[0], t_r_last[1]) for _ in range(2)]
    IS8 = wdi
    IS9 = view(S9, 0, I32, [128, D])
    ISG = view(S10, 0, I32, [128, D])

    def hid_ap(hc):
        return (hidA, hidB, hidC)[hc // 8][:, hc % 8, :]


    t_is8_free = t_wgi_free
    t_isg_free = None
    t_is9_free = tmax(t_ln1s_free[0], t_ln1s_free[1])
    for hc in range(HCN):
        par = hc % 2
        dg = b.emit("sync", lambda e, hc=hc: e.dma_start(
            out=ISG, in_=wg_d[128 * hc:128 * (hc + 1), :]),
            waits=tmax(t_isg_free), inc="I8", dma=True)
        cg = b.emit("vector", lambda e: e.tensor_copy(out=wgc, in_=ISG),
                    waits=tmax(dg, t_wgc_free), inc="VE")
        t_isg_free = cg
        du = b.emit("sync", lambda e, hc=hc: e.dma_start(
            out=IS9, in_=wu_d[128 * hc:128 * (hc + 1), :]),
            waits=tmax(t_is9_free), inc="I9", dma=True)
        cu = b.emit("vector", lambda e: e.tensor_copy(out=wuc, in_=IS9),
                    waits=tmax(du, t_wuc_free), inc="VE")
        t_is9_free = cu
        bkg, bku = par, 2 + par
        ttg = None
        for j in range(8):
            ttg = b.emit("tensor", lambda e, j=j, bkg=bkg: e.transpose(
                psAb[:, 1024 * bkg + 128 * j:1024 * bkg + 128 * (j + 1)],
                wgc[:, 128 * j:128 * (j + 1)], ident),
                waits=tmax(cg, t_trA[bkg]) if j == 0 else (),
                inc="TE" if j == 7 else None)
        evg = b.emit("vector", lambda e, bkg=bkg, par=par: e.tensor_copy(
            out=wgT[par].rearrange("p c k -> p (c k)"),
            in_=psAb[:, 1024 * bkg:1024 * (bkg + 1)]),
            waits=tmax(ttg, t_wgT_used[par]), inc="VE")
        t_trA[bkg] = tmax(evg)
        ttu = None
        for j in range(8):
            ttu = b.emit("tensor", lambda e, j=j, bku=bku: e.transpose(
                psAb[:, 1024 * bku + 128 * j:1024 * bku + 128 * (j + 1)],
                wuc[:, 128 * j:128 * (j + 1)], ident),
                waits=tmax(cu, t_trA[bku]) if j == 0 else (),
                inc="TE" if j == 7 else None)
        evu = b.emit("vector", lambda e, bku=bku, par=par: e.tensor_copy(
            out=wuT[par].rearrange("p c k -> p (c k)"),
            in_=psAb[:, 1024 * bku:1024 * (bku + 1)]),
            waits=tmax(ttu, t_wuT_used[par]), inc="VE")
        t_trA[bku] = tmax(evu)
        mmg_last, mmu_last = None, None
        for q4 in range(4):
            gp_ = q4 % 2
            mmg = None
            for kc in range(DKC):
                w0 = tmax(evg, t_gslot[gp_], t_ln1f_all) if kc == 0 else ()
                mmg = b.emit("tensor", lambda e, par=par, kc=kc, q4=q4, gp_=gp_: e.matmul(
                    psB[:, 512 * gp_:512 * (gp_ + 1)], wgT[par][:, kc, :],
                    ln1f[:, kc, 512 * q4:512 * (q4 + 1)],
                    start=(kc == 0), stop=(kc == DKC - 1)),
                    waits=w0, inc="TE" if kc == DKC - 1 else None)
            sg = b.emit("scalar", lambda e, gp_=gp_: e.activation(
                out=gs[gp_], in_=psB[:, 512 * gp_:512 * (gp_ + 1)], func=AF.Silu, scale=DQ),
                waits=tmax(mmg, t_gs_free[gp_]), inc="SE")
            t_gslot[gp_] = sg
            mmu = None
            for kc in range(DKC):
                w0 = tmax(evu, t_uslot[gp_], t_ln1f_all) if kc == 0 else ()
                mmu = b.emit("tensor", lambda e, par=par, kc=kc, q4=q4, gp_=gp_: e.matmul(
                    psC[:, 512 * gp_:512 * (gp_ + 1)], wuT[par][:, kc, :],
                    ln1f[:, kc, 512 * q4:512 * (q4 + 1)],
                    start=(kc == 0), stop=(kc == DKC - 1)),
                    waits=w0, inc="TE" if kc == DKC - 1 else None)
            cu2 = b.emit("vector", lambda e, gp_=gp_: e.tensor_scalar(
                out=us[gp_], in0=psC[:, 512 * gp_:512 * (gp_ + 1)],
                scalar1=DQ, scalar2=None, op0=ALU.mult),
                waits=tmax(mmu, t_us_free[gp_]), inc="VE")
            t_uslot[gp_] = cu2
            hm = b.emit("vector", lambda e, hc=hc, q4=q4, gp_=gp_: e.tensor_tensor(
                out=hid_ap(hc)[:, 512 * q4:512 * (q4 + 1)], in0=gs[gp_], in1=us[gp_],
                op=ALU.mult), waits=tmax(sg, cu2), inc="VE")
            t_gs_free[gp_] = tmax(hm)
            t_us_free[gp_] = tmax(hm)
            t_hid_toks.append(hm)
            mmg_last, mmu_last = mmg, mmu
        t_wgT_used[par] = tmax(mmg_last)
        t_wuT_used[par] = tmax(mmu_last)
        dc, third = hc // 3, hc % 3
        dd = b.emit("sync", lambda e, dc=dc, third=third: e.dma_start(
            out=IS8, in_=wd_d[128 * dc:128 * (dc + 1), 1024 * third:1024 * (third + 1)]),
            waits=tmax(t_is8_free), inc="ID", dma=True)
        cd = b.emit("vector", lambda e: e.tensor_copy(out=wdc, in_=IS8),
                    waits=tmax(dd, ttg), inc="VE")
        t_is8_free = cd
        ttd = None
        for j in range(8):
            ttd = b.emit("tensor", lambda e, j=j, bkg=bkg: e.transpose(
                psAb[:, 1024 * bkg + 128 * j:1024 * bkg + 128 * (j + 1)],
                wdc[:, 128 * j:128 * (j + 1)], ident),
                waits=tmax(cd, t_trA[bkg]) if j == 0 else (),
                inc="TE" if j == 7 else None)
        if third < 2:
            ddst = wdTa[:, 8 * third:8 * third + 8, 128 * dc:128 * (dc + 1)]
        else:
            ddst = wdTb[:, 0:8, 128 * dc:128 * (dc + 1)]
        dwait = tmax(t_kq_used[0], t_kq_used[1])
        evd = b.emit("vector", lambda e, ddst=ddst, bkg=bkg: e.tensor_copy(
            out=ddst, in_=psAb[:, 1024 * bkg:1024 * (bkg + 1)].rearrange(
                "p (c k) -> p c k", c=8)),
            waits=tmax(ttd, dwait), inc="VE")
        t_trA[bkg] = tmax(evd)
        t_wgc_free = tmax(ttd)
        t_wuc_free = tmax(ttu)
        t_wdT_toks.append(evd)
    t_hid_all = tmax(*t_hid_toks)
    t_wdT_all = tmax(*t_wdT_toks)

    dslots = [psB[:, 0:512], psB[:, 512:1024], psC[:, 0:512], psC[:, 512:1024]]
    t_dfree = [t_gslot[0], t_gslot[1], t_uslot[0], t_uslot[1]]
    t_ost_free = [None, None]
    t_res2_free = [None, tmax(t_gs_free[0], t_gs_free[1], t_us_free[0], t_us_free[1])]
    t_s2_prev5 = [tmax(t_ph3_se), tmax(t_ph3_se)]
    t_ln_prev5 = [None, None]
    t_ln1tc_free = [tmax(t_is9_free), tmax(t_is9_free)]
    t_lnT_bank5 = [t_trA[i] for i in range(4)]
    si = 0
    t_out_toks = []
    for tc in range(TCN):
        sl = tc % 2
        bank = tc % 4
        res2 = res2p[sl]
        ln1tc = ln1tcp[sl]
        tt = None
        for kc in range(DKC):
            tt = b.emit(
                "tensor", lambda e, tc=tc, kc=kc, bank=bank: e.transpose(
                    psAb[:, 1024 * bank + 128 * kc:1024 * bank + 128 * (kc + 1)],
                    ln1f[:, kc, 128 * tc:128 * (tc + 1)], ident),
                waits=tmax(t_ln1f_all, t_lnT_bank5[bank]) if kc == 0 else (),
                inc="TE" if kc == DKC - 1 else None)
        psrc = psAb[:, 1024 * bank:1024 * (bank + 1)]
        evt = b.emit(
            "vector", lambda e, psrc=psrc, ln1tc=ln1tc: e.tensor_copy(
                out=ln1tc, in_=psrc),
            waits=tmax(tt, t_ln1tc_free[sl]), inc="VE")
        t_lnT_bank5[bank] = tmax(evt)
        t_r2 = None
        for dn in range(2):
            psl = si % 4
            si += 1
            mm = None
            for hc in range(HCN):
                w0 = ()
                if hc == 0:
                    w0 = tmax(t_dfree[psl],
                              t_hid_all if (tc == 0 and dn == 0) else None,
                              t_wdT_all if (tc == 0 and dn == 0) else None)
                wsrc = wdTa[:, hc, 512 * dn:512 * (dn + 1)] if hc < 16 else \
                    wdTb[:, hc - 16, 512 * dn:512 * (dn + 1)]
                mm = b.emit("tensor", lambda e, tc=tc, hc=hc, psl=psl, wsrc=wsrc: e.matmul(
                    dslots[psl], hid_ap(hc)[:, 128 * tc:128 * (tc + 1)], wsrc,
                    start=(hc == 0), stop=(hc == HCN - 1)),
                    waits=w0, inc="TE" if hc == HCN - 1 else None)
            t_r2 = b.emit("vector", lambda e, dn=dn, psl=psl, res2=res2, ln1tc=ln1tc: e.scalar_tensor_tensor(
                out=res2[:, 512 * dn:512 * (dn + 1)], in0=dslots[psl], scalar=DQ,
                in1=ln1tc[:, 512 * dn:512 * (dn + 1)], op0=ALU.mult, op1=ALU.add),
                waits=tmax(mm, evt, t_res2_free[sl],
                           tmax(*t_astg_free[0], *t_astg_free[1]) if tc == 0 and dn == 0 else None),
                inc="VE")
            t_dfree[psl] = t_r2
        t_ln1tc_free[sl] = t_r2
        osl = tc % 2
        t_ln2, t_s1t5, t_s2t5 = ln_chain(
            sl, res2, t_r2, outst[osl], [t_s2_prev5[sl], t_ost_free[osl]],
            outst[osl], [t_ost_free[osl]])
        t_s2_prev5[sl] = t_s2t5
        t_ln_prev5[sl] = t_ln2
        t_res2_free[sl] = t_ln2
        od = b.emit("sync", lambda e, tc=tc, osl=osl: e.dma_start(
            out=out_d[128 * tc:128 * (tc + 1), :], in_=outst[osl]),
            waits=tmax(t_ln2), inc="OD%d" % osl, dma=True)
        t_ost_free[osl] = od
        t_out_toks.append(od)

    if DBG:
        taps = [
            ("dbg_q", Qp.rearrange("p c t -> p (c t)"), tmax(t_q_ready)),
            ("dbg_k", Kp.rearrange("p c t -> p (c t)"), tmax(t_k_ready)),
            ("dbg_v", Vtok.rearrange("p c t -> p (c t)"), tmax(t_v_ready)),
            ("dbg_attn", attn.rearrange("p c t -> p (c t)"), t_attn_all),
            ("dbg_ln1f", ln1f.rearrange("p c t -> p (c t)"), t_ln1f_all),
            ("dbg_hida", hidA.rearrange("p c t -> p (c t)"), t_hid_all),
            ("dbg_xt", xT.rearrange("p c t -> p (c t)"), tmax(t_xT_last)),
        ]
        for nm, src, wv_ in taps:
            tok = b.emit("gpsimd", lambda e, nm=nm, src=src: e.dma_start(
                out=dbg_d[nm], in_=src), waits=wv_, inc="GD", dma=True)
            t_out_toks.append(tok)

    b.emit("sync", lambda e: e.nop(), waits=tmax(*t_out_toks), keep=True)

    import contextlib
    with contextlib.ExitStack() as ctx:
        sems = {nm: ctx.enter_context(nc.semaphore("s" + nm)) for nm in STREAMS}
        with nc.Block() as block:
            b.flush(block, sems)
    return nc


def _consts():
    ratio = 2.0 ** (-8.0 / NH)
    sc = np.asarray([ratio ** (i + 1) for i in range(NH)],
                    dtype=np.float64).astype(np.float32).reshape(NH, 1)
    jr = np.tile(np.arange(T, dtype=np.float32), (NH, 1))
    p = np.arange(128)
    maskm = np.where(p[None, :] >= p[:, None], 0.0, -30000.0).astype(np.float32)
    ident = np.eye(128).astype(ml_dtypes.bfloat16)
    return jr, sc, maskm, ident


def kernel(L0_attn_x, L0_attn_q_w, L0_attn_k_w, L0_attn_v_w, L0_attn_o_w,
           L0_ffn_w_gate, L0_ffn_w_up, L0_ffn_w_down):
    x = np.ascontiguousarray(np.asarray(L0_attn_x, dtype=np.float32))
    wq = np.ascontiguousarray(np.asarray(L0_attn_q_w, dtype=np.float32))
    wk = np.ascontiguousarray(np.asarray(L0_attn_k_w, dtype=np.float32))
    wv = np.ascontiguousarray(np.asarray(L0_attn_v_w, dtype=np.float32))
    wo = np.ascontiguousarray(np.asarray(L0_attn_o_w, dtype=np.float32))
    wg = np.ascontiguousarray(np.asarray(L0_ffn_w_gate, dtype=np.int32))
    wu = np.ascontiguousarray(np.asarray(L0_ffn_w_up, dtype=np.int32))
    wd = np.ascontiguousarray(np.asarray(L0_ffn_w_down, dtype=np.int32))
    jr, sc, maskm, ident = _consts()

    nc = build_nc()
    in_maps = []
    for i in range(8):
        in_maps.append({
            "x": x[i], "wq": wq, "wk": wk, "wv": wv, "wo": wo,
            "wg": wg, "wu": wu, "wd": wd,
            "jr": jr, "sc": sc, "maskm": maskm, "ident": ident,
        })
    res = run_bass_kernel_spmd(nc, in_maps, core_ids=list(range(8)))
    out = np.stack([np.asarray(r["out"], dtype=np.float32) for r in res.results], axis=0)
    return out
